# version 12
# baseline (speedup 1.0000x reference)
import numpy as np
import ml_dtypes
import concourse.bass as bass
import concourse.mybir as mybir
from concourse.bass_utils import run_bass_kernel_spmd

dt = mybir.dt
AF = mybir.ActivationFunctionType
ALU = mybir.AluOpType
AX = mybir.AxisListType
BF16 = ml_dtypes.bfloat16

D = 1024
SQ = 4096
TOK = 2048
NH = 16
HD = 64
NE = 16
CH = 8
NKT = 32
NTT = 16
NPAIR = 8
NBLK = 2
BT = NTT // NBLK
EPS_LN = 1e-5
EPS_MOE = 1e-10


class Ctr:
    def __init__(self):
        self.v = 0


def build():
    NKT = SQ // 128
    NTT = TOK // 128
    BT = NTT // NBLK
    nc = bass.Bass("TRN2", target_bir_lowering=False)

    d_xfm = nc.dram_tensor("xfm", [D, SQ], dt.float32, kind="ExternalInput").ap()
    d_xqf = nc.dram_tensor("xqf", [D, TOK], dt.float32, kind="ExternalInput").ap()
    d_xtm = nc.dram_tensor("xtm", [TOK, D], dt.float32, kind="ExternalInput").ap()
    d_wqT = nc.dram_tensor("wqT", [D, D], dt.float32, kind="ExternalInput").ap()
    d_wkT = nc.dram_tensor("wkT", [D, D], dt.float32, kind="ExternalInput").ap()
    d_wvT = nc.dram_tensor("wvT", [D, D], dt.float32, kind="ExternalInput").ap()
    d_rwT = nc.dram_tensor("rwT", [D, NE], dt.float32, kind="ExternalInput").ap()
    d_rb = nc.dram_tensor("rb", [128, NE], dt.float32, kind="ExternalInput").ap()
    d_ew = nc.dram_tensor("ew", [NE, D, D], dt.float32, kind="ExternalInput").ap()
    d_idb = nc.dram_tensor("idb", [128, 128], dt.bfloat16, kind="ExternalInput").ap()
    d_idf = nc.dram_tensor("idf", [128, 128], dt.float32, kind="ExternalInput").ap()
    d_oz = nc.dram_tensor("oz", [128, 64], dt.bfloat16, kind="ExternalInput").ap()
    d_y = nc.dram_tensor("y", [TOK, D], dt.float32, kind="ExternalOutput").ap()
    d_ewb = nc.dram_tensor("ewb", [NE, D, D], dt.bfloat16).ap()

    v_xfm = d_xfm.rearrange("(c p) t -> p c t", c=CH)
    v_xqf = d_xqf.rearrange("(c p) t -> p c t", c=CH)
    v_wT = {"K": d_wkT.rearrange("(c p) j -> p c j", c=CH),
            "V": d_wvT.rearrange("(c p) j -> p c j", c=CH),
            "Q": d_wqT.rearrange("(c p) j -> p c j", c=CH)}
    v_rwT = d_rwT.rearrange("(c p) e -> p c e", c=CH)

    cms = []

    def alloc(name, shape, dtype, psum=False):
        cm = nc.psum_tensor(name, shape, dtype) if psum else nc.sbuf_tensor(name, shape, dtype)
        t = cm.__enter__()
        cms.append(cm)
        return t

    idb = alloc("s_idb", [128, 128], dt.bfloat16)
    idf = alloc("s_idf", [128, 128], dt.float32)
    ozb = alloc("s_ozb", [128, 64], dt.bfloat16)
    rbr = alloc("s_rbr", [128, NE], dt.float32)
    rwTb = alloc("s_rwTb", [128, CH * NE], dt.bfloat16)
    cb3 = alloc("s_cb3", [128, 1], dt.float32)
    rzpre = alloc("s_rzpre", [128, NTT * NH], dt.float32)
    rz = alloc("s_rz", [128, NTT * NH], dt.float32)
    gates = alloc("s_gates", [128, NTT * NE], dt.float32)
    eps2 = alloc("s_eps2", [128, NTT], dt.float32)
    tA = alloc("s_tA", [128, 1], dt.float32)
    tB = alloc("s_tB", [128, 1], dt.float32)
    tC = alloc("s_tC", [128, 1], dt.float32)
    tD = alloc("s_tD", [128, 1], dt.float32)
    tE = alloc("s_tE", [128, 1], dt.float32)
    QO = alloc("s_QO", [128, CH * TOK], dt.bfloat16)
    RA = alloc("s_RA", [128, CH * SQ], dt.bfloat16)
    RB = alloc("s_RB", [128, NKT * D], dt.bfloat16)
    RC = alloc("s_RC", [128, 16384], dt.bfloat16)
    PS = alloc("s_PS", [128, 4096], dt.float32, psum=True)

    RAf = RA[:, :].bitcast(dt.float32)
    RBf = RB[:, :].bitcast(dt.float32)

    def bank(i, lo=0, hi=512):
        return PS[:, i * 512 + lo: i * 512 + hi]

    def xstage(par):
        return RC[:, par * 4096:(par + 1) * 4096]

    wT = RC[:, 8192:16384]

    def pbuf(par):
        return RC[:, par * 2048:(par + 1) * 2048]

    zs = RC[:, 12288:14336].bitcast(dt.float32)

    def x0stage(par):
        return RC[:, 8192 + par * 2048: 8192 + (par + 1) * 2048].bitcast(dt.float32)

    u_f32 = RC[:, 4096:6144].bitcast(dt.float32)
    usq_f32 = RC[:, 6144:8192].bitcast(dt.float32)
    x1b = RC[:, 14336:15360]
    e16 = RC[:, 15360:15392].bitcast(dt.float32)
    lgt = RC[:, 15392:15424].bitcast(dt.float32)
    mx8 = RC[:, 15424:15440].bitcast(dt.float32)
    sel = RC[:, 15440:15472].bitcast(dt.float32)
    t16 = RC[:, 15472:15504].bitcast(dt.float32)
    t16b = RC[:, 15504:15536].bitcast(dt.float32)

    def ot32(par):
        return RC[:, par * 2048:(par + 1) * 2048].bitcast(dt.float32)

    def accA(tt):
        return RAf[:, 8192 + tt * 1024: 8192 + (tt + 1) * 1024]

    def accB(tt):
        return RBf[:, 8192 + tt * 1024: 8192 + (tt + 1) * 1024]

    def x1slot(tt):
        return RAf[:, tt * 1024: (tt + 1) * 1024]

    def wbuf(par):
        return RB[:, par * 8192:(par + 1) * 8192]

    sem_cms = []

    def mksem(name):
        cm = nc.semaphore(name)
        s = cm.__enter__()
        sem_cms.append(cm)
        return s

    gC = mksem("gC")
    gEW = mksem("gEW")
    gX = [mksem("gX0"), mksem("gX1")]
    gW = [mksem("gW0"), mksem("gW1")]
    sX = [mksem("sX0"), mksem("sX1")]
    sO = [mksem("sO0"), mksem("sO1")]
    tS = mksem("tS")
    tV = mksem("tV")
    sE = mksem("sE")
    vE = mksem("vE")
    cgC, ctS, ctV, csE, cvE = (Ctr() for _ in range(5))
    cgX = [Ctr(), Ctr()]
    cgW = [Ctr(), Ctr()]
    csX = [Ctr(), Ctr()]
    csO = [Ctr(), Ctr()]

    SC = {k: [] for k in ("sync", "gpsimd", "tensor", "vector", "scalar")}

    def q(eng, fn):
        SC[eng].append(fn)

    for (dst, src) in ((idb[:, :], d_idb), (idf[:, :], d_idf), (ozb[:, :], d_oz),
                       (rbr[:, :], d_rb),
                       (rwTb[:, :].rearrange("p (c e) -> p c e", c=CH), v_rwT)):
        def f(g, dst=dst, src=src):
            g.dma_start(out=dst, in_=src).then_inc(gC, 16)
        q("gpsimd", f)
        cgC.v += 16

    consts_done = cgC.v

    def f(v):
        v.wait_ge(gC, consts_done)
        v.memset(cb3[:, :], -3.0)
        v.memset(rzpre[:, :], 0.0)
    q("vector", f)

    def proj_pass(kind, nsect):
        if kind == "V":
            wtile = QO[:, 0:8192]
            wt_gate = 0
        else:
            wtile = wT
            wt_gate = ctS.v if kind == "Q" else 0

        def fw(g, wsrc=v_wT[kind], wt_gate=wt_gate, wtile=wtile):
            if wt_gate > 0:
                g.wait_ge(tS, wt_gate)
            g.dma_start(out=wtile.rearrange("p (c j) -> p c j", c=CH),
                        in_=wsrc).then_inc(gW[0], 16)
        q("gpsimd", fw)
        cgW[0].v += 16
        w_done = cgW[0].v
        xsrc = v_xqf if kind == "Q" else v_xfm

        for s in range(nsect):
            par = s % 2
            ts_need = max(0, ctS.v - 8)

            def fx(g, s=s, par=par, ts_need=ts_need, xsrc=xsrc):
                if ts_need > 0:
                    g.wait_ge(tS, ts_need)
                g.dma_start(out=xstage(par).rearrange("p (c t) -> p c t", c=CH),
                            in_=xsrc[:, :, s * 512:(s + 1) * 512]).then_inc(gX[par], 16)
            q("gpsimd", fx)
            cgX[par].v += 16
            x_done = cgX[par].v

            for u in range(8):
                b = ctS.v % 4
                ve_need = ctS.v - 3

                if kind in ("K", "Q"):
                    jc = u

                    def ft(t, jc=jc, b=b, par=par, x_done=x_done, w_done=w_done,
                           ve_need=ve_need):
                        t.wait_ge(gW[0], w_done)
                        t.wait_ge(gX[par], x_done)
                        if ve_need > 0:
                            t.wait_ge(vE, ve_need)
                        for dc in range(CH):
                            ins = t.matmul(
                                bank(b),
                                wtile[:, dc * 1024 + jc * 128: dc * 1024 + (jc + 1) * 128],
                                xstage(par)[:, dc * 512:(dc + 1) * 512],
                                start=(dc == 0), stop=(dc == CH - 1))
                        ins.then_inc(tS, 1)
                    if kind == "K":
                        dst = RA[:, jc * SQ + s * 512: jc * SQ + (s + 1) * 512]
                    else:
                        dst = QO[:, jc * TOK + s * 512: jc * TOK + (s + 1) * 512]
                else:
                    tt, jh = u // 2, u % 2
                    kt = s * 4 + tt

                    def ft(t, tt=tt, jh=jh, b=b, par=par, x_done=x_done,
                           w_done=w_done, ve_need=ve_need):
                        t.wait_ge(gW[0], w_done)
                        t.wait_ge(gX[par], x_done)
                        if ve_need > 0:
                            t.wait_ge(vE, ve_need)
                        for dc in range(CH):
                            ins = t.matmul(
                                bank(b),
                                xstage(par)[:, dc * 512 + tt * 128:
                                            dc * 512 + (tt + 1) * 128],
                                wtile[:, dc * 1024 + jh * 512: dc * 1024 + (jh + 1) * 512],
                                start=(dc == 0), stop=(dc == CH - 1))
                        ins.then_inc(tS, 1)
                    dst = RB[:, kt * D + jh * 512: kt * D + (jh + 1) * 512]

                q("tensor", ft)
                ctS.v += 1
                te_after = ctS.v

                def fv(v, dst=dst, b=b, te_after=te_after):
                    v.wait_ge(tS, te_after)
                    v.tensor_copy(dst, bank(b)).then_inc(vE, 1)
                q("vector", fv)
                cvE.v += 1

    proj_pass("K", SQ // 512)
    proj_pass("V", SQ // 512)
    proj_pass("Q", TOK // 512)
    p1_ve = cvE.v
    p1_ts = ctS.v

    for e in range(NE):
        def few(g, e=e):
            g.dma_start(out=d_ewb[e], in_=d_ew[e]).then_inc(gEW, 16)
        q("gpsimd", few)
    ew_cast_done = NE * 16

    fve_cnt = {}
    fvz_cnt = {}
    pv_cnt = {}
    z_cnt = {}

    def emit_pv(t, c, kcp, first, last, p_idx=0):
        P = pbuf((32 * p_idx + kcp) % 6)
        for head in range(2):
            hh = 2 * c + head
            orow = slice(head * 64, (head + 1) * 64)
            for qc in range(2):
                ins = t.matmul(
                    PS[orow, 2048 + qc * 512: 2048 + (qc + 1) * 512],
                    RB[:, kcp * D + hh * 64: kcp * D + (hh + 1) * 64],
                    P[:, head * 1024 + qc * 512: head * 1024 + (qc + 1) * 512],
                    start=first, stop=last, tile_position=(0, head * 64),
                    skip_group_check=True)
        return ins

    def emit_z(t, c, kcp, first, last, p_idx=0):
        P = pbuf((32 * p_idx + kcp) % 6)
        for head in range(2):
            orow = slice(head * 64, (head + 1) * 64)
            for qc in range(2):
                ins = t.matmul(
                    PS[orow, 3072 + qc * 512: 3072 + (qc + 1) * 512],
                    ozb[:, :],
                    P[:, head * 1024 + qc * 512: head * 1024 + (qc + 1) * 512],
                    start=first, stop=last, tile_position=(0, head * 64),
                    skip_group_check=True)
        return ins

    def emit_ftz(p_prev):
        def ftz(t, zev=fve_cnt[p_prev]):
            t.wait_ge(vE, zev)
            for sl in range(8):
                ins = t.transpose(PS[:, 3072 + sl * 128: 3072 + (sl + 1) * 128],
                                  zs[:, sl * 128:(sl + 1) * 128], idf[:, :])
            ins.then_inc(tV, 1)
        q("tensor", ftz)
        ctV.v += 1
        ztr = ctV.v

        def fvz(v, c=p_prev % NPAIR, h=p_prev // NPAIR, ztr=ztr):
            v.wait_ge(tV, ztr)
            for sl in range(8):
                tile = h * 8 + sl
                v.tensor_copy(rzpre[:, tile * NH + 2 * c: tile * NH + 2 * c + 1],
                              PS[:, 3072 + sl * 128: 3072 + sl * 128 + 1])
                ins = v.tensor_copy(
                    rzpre[:, tile * NH + 2 * c + 1: tile * NH + 2 * c + 2],
                    PS[:, 3072 + sl * 128 + 64: 3072 + sl * 128 + 65])
            ins.then_inc(vE, 1)
        q("vector", fvz)
        cvE.v += 1
        fvz_cnt[p_prev] = cvE.v

    pvl_exp_need = {}

    def emit_pvl(p_prev):
        cc = p_prev % NPAIR

        def ftPVl(t, cc=cc, exp_need=pvl_exp_need[p_prev]):
            t.wait_ge(sE, exp_need)
            emit_pv(t, cc, NKT - 1, False, True, p_prev).then_inc(tV, 1)
        q("tensor", ftPVl)
        ctV.v += 1
        pv_cnt[(p_prev, NKT - 1)] = ctV.v

    def emit_zl(p_prev):
        cc = p_prev % NPAIR

        def ftZl(t, cc=cc, p_prev=p_prev):
            for kcp in (NKT - 3, NKT - 2, NKT - 1):
                ins = emit_z(t, cc, kcp, False, kcp == NKT - 1, p_prev)
            ins.then_inc(tV, 1)
        q("tensor", ftZl)
        ctV.v += 1
        for kcp in (NKT - 3, NKT - 2, NKT - 1):
            z_cnt[(p_prev, kcp)] = ctV.v
        pvz_done = ctV.v
        qc0p = (p_prev // NPAIR) * 1024

        def fve(v, cc=cc, qc0=qc0p, pvz_done=pvz_done):
            v.wait_ge(tV, pvz_done)
            v.tensor_copy(QO[:, cc * TOK + qc0: cc * TOK + qc0 + 1024],
                          PS[:, 2048:3072])
            ins = v.tensor_copy(zs[:, :], PS[:, 3072:4096])
            ins.then_inc(vE, 1)
        q("vector", fve)
        cvE.v += 1
        fve_cnt[p_prev] = cvE.v

    for h in range(2):
        for c in range(NPAIR):
            p = h * NPAIR + c
            qc0 = h * 1024

            for kc in range(NKT):
                exp_need = csE.v
                first_pair_wait = p1_ve if (p == 0 and kc == 0) else 0

                def ftS(t, c=c, kc=kc, qc0=qc0, exp_need=exp_need,
                        fw=first_pair_wait):
                    if fw > 0:
                        t.wait_ge(vE, fw)
                        t.wait_ge(gC, consts_done)
                    for head in range(2):
                        t.wait_ge(sE, max(0, exp_need - 1 + head))
                        rows = slice(head * 64, (head + 1) * 64)
                        for qc in range(2):
                            ins = t.matmul(
                                PS[:, head * 1024 + qc * 512:
                                   head * 1024 + (qc + 1) * 512],
                                RA[rows, c * SQ + kc * 128: c * SQ + (kc + 1) * 128],
                                QO[rows, c * TOK + qc0 + qc * 512:
                                   c * TOK + qc0 + (qc + 1) * 512],
                                start=True, stop=True, tile_position=(head * 64, 0))
                        if head == 0:
                            ins.then_inc(tS, 1)
                    ins.then_inc(tS, 1)
                q("tensor", ftS)
                ctS.v += 2
                s_done = ctS.v

                if p > 0:
                    if kc == 1:
                        emit_pvl(p - 1)
                    elif kc == 2:
                        emit_zl(p - 1)
                    elif kc == 4:
                        emit_ftz(p - 1)

                if kc >= 5:
                    kcz = kc - 5
                    ve_need = fvz_cnt.get(p - 1, 0) if kcz == 0 else 0

                    def ftZ(t, c=c, kcz=kcz, ve_need=ve_need, p=p):
                        if ve_need > 0:
                            t.wait_ge(vE, ve_need)
                        emit_z(t, c, kcz, kcz == 0, False, p).then_inc(tV, 1)
                    q("tensor", ftZ)
                    ctV.v += 1
                    z_cnt[(p, kcz)] = ctV.v

                if kc >= 3:
                    kcp = kc - 3
                    ve_need = fve_cnt.get(p - 1, 0) if kcp == 0 else 0

                    def ftPV(t, c=c, kcp=kcp, exp_need=max(0, exp_need - 4),
                             ve_need=ve_need, p=p):
                        t.wait_ge(sE, exp_need)
                        if ve_need > 0:
                            t.wait_ge(vE, ve_need)
                        emit_pv(t, c, kcp, kcp == 0, False, p).then_inc(tV, 1)
                    q("tensor", ftPV)
                    ctV.v += 1
                    pv_cnt[(p, kcp)] = ctV.v

                def fse(sc, s_done=s_done, kc=kc, p=p,
                        pv_need=(z_cnt.get((p, kc - 6), 0) if kc >= 6
                                 else z_cnt.get((p - 1, kc + 26), 0))):
                    if pv_need > 0:
                        sc.wait_ge(tV, pv_need)
                    P = pbuf((32 * p + kc) % 6)
                    sc.wait_ge(tS, s_done - 1)
                    sc.activation(P[:, 0:1024], PS[:, 0:1024], AF.Exp,
                                  bias=cb3[:, :], scale=0.125).then_inc(sE, 1)
                    sc.wait_ge(tS, s_done)
                    sc.activation(P[:, 1024:2048], PS[:, 1024:2048], AF.Exp,
                                  bias=cb3[:, :], scale=0.125).then_inc(sE, 1)
                q("scalar", fse)
                csE.v += 2

            for kcp in (NKT - 3, NKT - 2):
                def ftPVt(t, c=c, kcp=kcp, exp_need=csE.v, p=p):
                    t.wait_ge(sE, exp_need)
                    emit_pv(t, c, kcp, False, False, p).then_inc(tV, 1)
                q("tensor", ftPVt)
                ctV.v += 1
                pv_cnt[(p, kcp)] = ctV.v
            for kcz in (NKT - 5, NKT - 4):
                def ftZt(t, c=c, kcz=kcz, p=p):
                    emit_z(t, c, kcz, False, False, p).then_inc(tV, 1)
                q("tensor", ftZt)
                ctV.v += 1
                z_cnt[(p, kcz)] = ctV.v
            pvl_exp_need[p] = csE.v

    emit_pvl(2 * NPAIR - 1)
    emit_zl(2 * NPAIR - 1)
    emit_ftz(2 * NPAIR - 1)


    p2_ve = cvE.v
    p2_tv = ctV.v

    def fvr(v, p2_ve=p2_ve):
        v.wait_ge(vE, p2_ve)
        v.reciprocal(rz[:, :], rzpre[:, :]).then_inc(vE, 1)
    q("vector", fvr)
    cvE.v += 1

    fvu_counts = {}
    fvl_counts = {}
    fvt_counts = {}
    ffn_mm_last = {}
    ffn_pb_last = {}

    def p3_tile(blk, tt, mid=None):
        t_i = blk * BT + tt
        hh = t_i // 8
        par = t_i % 2
        pcol = par * 1024

        if t_i >= 2:
            ve_need = fvu_counts[t_i - 2]
        else:
            ve_need = 0
        ts_need = p1_ts if t_i < 2 else 0
        tv_need = p2_tv if t_i < 2 else 0

        def fx0(sy, t_i=t_i, par=par, ve_need=ve_need, ts_need=ts_need,
                tv_need=tv_need):
            if ts_need > 0:
                sy.wait_ge(tS, ts_need)
            if tv_need > 0:
                sy.wait_ge(tV, tv_need)
            if ve_need > 0:
                sy.wait_ge(vE, ve_need)
            sy.dma_start(out=x0stage(par),
                         in_=d_xtm[t_i * 128:(t_i + 1) * 128, :]).then_inc(sX[par], 16)
        q("sync", fx0)
        csX[par].v += 16
        x0_done = csX[par].v

        tr_need = fvl_counts.get(t_i - 2, p2_ve + 1 if t_i < 2 else 0)

        def ftr(t, t_i=t_i, pcol=pcol, tr_need=tr_need, first=(t_i == 0)):
            if first:
                t.wait_ge(vE, p2_ve + 1)
            if tr_need > 0:
                t.wait_ge(vE, tr_need)
            q_lo = t_i * 128
            for cc in range(CH):
                ins = t.matmul(PS[:, pcol + cc * 128: pcol + (cc + 1) * 128],
                               QO[:, cc * TOK + q_lo: cc * TOK + q_lo + 128],
                               idb[:, :], start=True, stop=True)
            ins.then_inc(tS, 1)
        q("tensor", ftr)
        ctS.v += 1
        tr_done = ctS.v
        if mid is not None:
            mid()

        def fvu(v, t_i=t_i, pcol=pcol, par=par, tr_done=tr_done, x0_done=x0_done):
            v.wait_ge(tS, tr_done)
            v.wait_ge(sX[par], x0_done)
            for sg in range(16):
                v.scalar_tensor_tensor(
                    u_f32[:, sg * 64:(sg + 1) * 64],
                    PS[:, pcol + sg * 64: pcol + (sg + 1) * 64],
                    rz[:, t_i * NH + sg: t_i * NH + sg + 1],
                    x0stage(par)[:, sg * 64:(sg + 1) * 64],
                    ALU.mult, ALU.add)
            v.drain()
            v.scalar_tensor_tensor(usq_f32[:, :], u_f32[:, :], 1.0, u_f32[:, :],
                                   ALU.mult, ALU.mult, accum_out=tA[:, :])
            v.reduce_sum(tB[:, :], u_f32[:, :], axis=AX.X)
            v.drain()
            v.tensor_scalar(tC[:, :], tB[:, :], 1.0 / D, None, ALU.mult)
            v.tensor_scalar(tD[:, :], tA[:, :], 1.0 / D, None, ALU.mult)
            v.drain()
            v.scalar_tensor_tensor(tE[:, :], tC[:, :], 1.0, tC[:, :],
                                   ALU.mult, ALU.mult)
            v.drain()
            v.tensor_tensor(tA[:, :], tD[:, :], tE[:, :], ALU.subtract)
            v.drain()
            v.tensor_scalar(tB[:, :], tA[:, :], EPS_LN, None,
                            ALU.add).then_inc(vE, 1)
        q("vector", fvu)
        cvE.v += 1
        fvu_counts[t_i] = cvE.v
        u_done = cvE.v

        def fsr(sc, u_done=u_done):
            sc.wait_ge(vE, u_done)
            sc.activation(tE[:, :], tB[:, :], AF.Ln, bias=0.0, scale=1.0)
            sc.drain()
            sc.activation(tD[:, :], tE[:, :], AF.Exp, bias=0.0,
                          scale=-0.5).then_inc(sE, 1)
        q("scalar", fsr)
        csE.v += 1
        r_done = csE.v

        def fva(v, tt=tt, r_done=r_done):
            v.wait_ge(sE, r_done)
            v.tensor_scalar(x1slot(tt), u_f32[:, :], tC[:, :], tD[:, :],
                            ALU.subtract, ALU.mult).then_inc(vE, 1)
        q("vector", fva)
        cvE.v += 1
        av_done = cvE.v

        def fsc(sc, tt=tt, av_done=av_done):
            sc.wait_ge(vE, av_done)
            sc.copy(x1b[:, :], x1slot(tt)).then_inc(sE, 1)
        q("scalar", fsc)
        csE.v += 1
        a_done = csE.v

        def ftx(t, pcol=pcol, a_done=a_done):
            t.wait_ge(sE, a_done)
            for cc in range(CH):
                ins = t.matmul(PS[:, pcol + cc * 128: pcol + (cc + 1) * 128],
                               x1b[:, cc * 128:(cc + 1) * 128], idb[:, :],
                               start=True, stop=True)
            ins.then_inc(tS, 1)
        q("tensor", ftx)
        ctS.v += 1
        x1t_done = ctS.v

        def fscx(sc, t_i=t_i, pcol=pcol, x1t_done=x1t_done):
            sc.wait_ge(tS, x1t_done)
            sc.copy(
                QO[:, :].rearrange("p (c t) -> p c t", c=CH)[:, :, t_i * 128:(t_i + 1) * 128],
                PS[:, pcol: pcol + 1024].rearrange("p (c t) -> p c t", c=CH),
            ).then_inc(sE, 1)
        q("scalar", fscx)
        csE.v += 1
        x1f_done = csE.v

        def ftro(t, t_i=t_i, pcol=pcol, x1f_done=x1f_done):
            t.wait_ge(sE, x1f_done)
            for dc in range(CH):
                ins = t.matmul(PS[:, pcol: pcol + NE],
                               QO[:, dc * TOK + t_i * 128: dc * TOK + (t_i + 1) * 128],
                               rwTb[:, dc * NE:(dc + 1) * NE],
                               start=(dc == 0), stop=(dc == CH - 1))
            ins.then_inc(tS, 1)
        q("tensor", ftro)
        ctS.v += 1
        ro_done = ctS.v

        def fvl(v, pcol=pcol, ro_done=ro_done):
            v.wait_ge(tS, ro_done)
            v.tensor_tensor(lgt[:, :], PS[:, pcol: pcol + NE], rbr[:, :], ALU.add)
            v.drain()
            v.reduce_max(tA[:, :], lgt[:, :], axis=AX.X,
                         negate=True).then_inc(vE, 1)
        q("vector", fvl)
        cvE.v += 1
        fvl_counts[t_i] = cvE.v
        l_done = cvE.v

        def fsx(sc, l_done=l_done):
            sc.wait_ge(vE, l_done)
            sc.activation(e16[:, :], lgt[:, :], AF.Exp, bias=tA[:, :],
                          scale=1.0, accum_out=tB[:, :]).then_inc(sE, 1)
        q("scalar", fsx)
        csE.v += 1
        e_done = csE.v

        def fvt(v, t_i=t_i, e_done=e_done):
            v.wait_ge(sE, e_done)
            v.reciprocal(tC[:, :], tB[:, :])
            v.max(mx8[:, :], e16[:, :])
            v.drain()
            v.tensor_scalar(t16[:, 0:6], mx8[:, 0:6], tC[:, :], None, ALU.mult)
            v.drain()
            v.reduce_sum(tA[:, :], t16[:, 0:6], axis=AX.X)
            v.drain()
            v.tensor_scalar(tB[:, :], tA[:, :], EPS_MOE, None, ALU.add)
            v.drain()
            v.reciprocal(tD[:, :], tB[:, :])
            v.drain()
            v.tensor_scalar(t16b[:, 0:6], t16[:, 0:6], tD[:, :], None, ALU.mult)
            v.drain()
            v.reduce_sum(tA[:, :], t16b[:, 0:6], axis=AX.X)
            v.drain()
            v.tensor_scalar(eps2[:, t_i:t_i + 1], tA[:, :], EPS_MOE, None,
                            ALU.mult)
            v.tensor_scalar(tB[:, :], tA[:, :], EPS_MOE, None, ALU.max)
            v.drain()
            v.reciprocal(tE[:, :], tB[:, :])
            v.drain()
            v.tensor_tensor(tA[:, :], tD[:, :], tE[:, :], ALU.mult)
            v.drain()
            v.tensor_tensor(tB[:, :], tA[:, :], tC[:, :], ALU.mult)
            v.tensor_scalar(sel[:, :], e16[:, :], mx8[:, 5:6], None, ALU.is_ge)
            v.drain()
            v.tensor_scalar(t16[:, 0:16], e16[:, :], tB[:, :], None, ALU.mult)
            v.drain()
            v.tensor_tensor(gates[:, t_i * NE:(t_i + 1) * NE], sel[:, :],
                            t16[:, 0:16], ALU.mult).then_inc(vE, 1)
        q("vector", fvt)
        cvE.v += 1
        fvt_counts[t_i] = cvE.v

    def ffn_wload(blk, e):
        gi = blk * NE + e
        if gi < 2:
            wl_tv, wl_ts = p2_tv, 0
        else:
            wl_tv, wl_ts = 0, ffn_mm_last[gi - 2]

        def fwl(g, e=e, wl_tv=wl_tv, wl_ts=wl_ts, par=gi % 2, first=(gi < 2)):
            if first:
                g.wait_ge(gEW, ew_cast_done)
            if wl_tv > 0:
                g.wait_ge(tV, wl_tv)
            if wl_ts > 0:
                g.wait_ge(tS, wl_ts)
            g.dma_start(out=wbuf(par).rearrange("p (c j) -> p c j", c=CH),
                        in_=d_ewb[e].rearrange("(c p) j -> p c j", c=CH)
                        ).then_inc(gW[par], 16)
        q("gpsimd", fwl)
        cgW[gi % 2].v += 16
        return cgW[gi % 2].v

    def ffn_unit(blk, e, tt, w_done):
        gi = blk * NE + e
        t_i = blk * BT + tt
        pb = 2048 + (tt % 2) * 1024
        ve_need = ffn_pb_last.get(tt % 2, 0)
        gate_need = fvt_counts[t_i] if e <= 1 else 0

        def ftf(t, t_i=t_i, pb=pb, w_done=w_done, first=(tt == 0),
                par=gi % 2, ve_need=ve_need, gate_need=gate_need):
            if first:
                t.wait_ge(gW[par], w_done)
            if gate_need > 0:
                t.wait_ge(vE, gate_need)
            if ve_need > 0:
                t.wait_ge(vE, ve_need)
            for jh in range(2):
                for dc in range(CH):
                    ins = t.matmul(
                        PS[:, pb + jh * 512: pb + (jh + 1) * 512],
                        QO[:, dc * TOK + t_i * 128: dc * TOK + (t_i + 1) * 128],
                        wbuf(par)[:, dc * 1024 + jh * 512: dc * 1024 + (jh + 1) * 512],
                        start=(dc == 0), stop=(dc == CH - 1))
            ins.then_inc(tS, 1)
        q("tensor", ftf)
        ctS.v += 1
        mm_done = ctS.v

        def fvf(v, e=e, tt=tt, t_i=t_i, pb=pb, mm_done=mm_done):
            v.wait_ge(tS, mm_done)
            g = gates[:, t_i * NE + e: t_i * NE + e + 1]
            if e == 0:
                v.tensor_scalar(accA(tt), PS[:, pb: pb + 1024], g, None,
                                ALU.mult).then_inc(vE, 1)
            elif e % 2 == 1:
                v.scalar_tensor_tensor(accB(tt), PS[:, pb: pb + 1024], g,
                                       accA(tt), ALU.mult, ALU.add).then_inc(vE, 1)
            else:
                v.scalar_tensor_tensor(accA(tt), PS[:, pb: pb + 1024], g,
                                       accB(tt), ALU.mult, ALU.add).then_inc(vE, 1)
        q("vector", fvf)
        cvE.v += 1
        ffn_pb_last[tt % 2] = cvE.v

    def ln2_tile(blk, tt):
        t_i = blk * BT + tt
        par = tt % 2
        sd_need = csO[par].v

        def fv2(v, tt=tt, t_i=t_i, par=par, sd_need=sd_need, sw=cvE.v):
            v.wait_ge(vE, sw)
            if sd_need > 0:
                v.wait_ge(sO[par], sd_need)
            v.scalar_tensor_tensor(u_f32[:, :], accB(tt),
                                   eps2[:, t_i:t_i + 1], x1slot(tt),
                                   ALU.add, ALU.add, accum_out=tB[:, :])
            v.drain()
            v.scalar_tensor_tensor(usq_f32[:, :], u_f32[:, :], 1.0, u_f32[:, :],
                                   ALU.mult, ALU.mult, accum_out=tA[:, :])
            v.drain()
            v.tensor_scalar(tC[:, :], tB[:, :], 1.0 / D, None, ALU.mult)
            v.tensor_scalar(tD[:, :], tA[:, :], 1.0 / D, None, ALU.mult)
            v.drain()
            v.scalar_tensor_tensor(tE[:, :], tC[:, :], 1.0, tC[:, :],
                                   ALU.mult, ALU.mult)
            v.drain()
            v.tensor_tensor(tA[:, :], tD[:, :], tE[:, :], ALU.subtract)
            v.drain()
            v.tensor_scalar(tB[:, :], tA[:, :], EPS_LN, None,
                            ALU.add).then_inc(vE, 1)
        q("vector", fv2)
        cvE.v += 1
        u2_done = cvE.v

        def fs2(sc, u2_done=u2_done):
            sc.wait_ge(vE, u2_done)
            sc.activation(tE[:, :], tB[:, :], AF.Ln, bias=0.0, scale=1.0)
            sc.drain()
            sc.activation(tD[:, :], tE[:, :], AF.Exp, bias=0.0,
                          scale=-0.5).then_inc(sE, 1)
        q("scalar", fs2)
        csE.v += 1
        r2_done = csE.v

        def fva2(v, par=par, r2_done=r2_done):
            v.wait_ge(sE, r2_done)
            v.tensor_scalar(ot32(par), u_f32[:, :], tC[:, :], tD[:, :],
                            ALU.subtract, ALU.mult).then_inc(vE, 1)
        q("vector", fva2)
        cvE.v += 1
        o_done = cvE.v

        def fso(sy, t_i=t_i, par=par, o_done=o_done):
            sy.wait_ge(vE, o_done)
            sy.dma_start(out=d_y[t_i * 128:(t_i + 1) * 128, :],
                         in_=ot32(par)).then_inc(sO[par], 16)
        q("sync", fso)
        csO[par].v += 16

    for blk in range(NBLK):
        w0 = ffn_wload(blk, 0)
        w1 = ffn_wload(blk, 1)
        for tt in range(BT):
            if tt == 0:
                p3_tile(blk, tt)
            else:
                p3_tile(blk, tt, mid=(lambda tp=tt - 1: (
                    ffn_unit(blk, 0, tp, w0), ffn_unit(blk, 1, tp, w1))))
        ffn_unit(blk, 0, BT - 1, w0)
        ffn_unit(blk, 1, BT - 1, w1)
        ffn_mm_last[blk * NE + 0] = ctS.v
        ffn_mm_last[blk * NE + 1] = ctS.v
        for e in range(2, NE):
            we = ffn_wload(blk, e)
            for tt in range(BT):
                ffn_unit(blk, e, tt, we)
                if e == NE - 1:
                    ln2_tile(blk, tt)
            ffn_mm_last[blk * NE + e] = ctS.v

    fin0, fin1 = csO[0].v, csO[1].v

    def ffin(sy, fin0=fin0, fin1=fin1):
        sy.wait_ge(sO[0], fin0)
        sy.wait_ge(sO[1], fin1)
    q("sync", ffin)

    def gfin(g, fin0=fin0, fin1=fin1):
        g.wait_ge(sO[0], fin0)
        g.wait_ge(sO[1], fin1)
    q("gpsimd", gfin)

    with nc.Block() as block:
        @block.sync
        def _(e):
            for fn in SC["sync"]:
                fn(e)

        @block.gpsimd
        def _(e):
            for fn in SC["gpsimd"]:
                fn(e)

        @block.tensor
        def _(e):
            for fn in SC["tensor"]:
                fn(e)

        @block.vector
        def _(e):
            for fn in SC["vector"]:
                fn(e)

        @block.scalar
        def _(e):
            for fn in SC["scalar"]:
                fn(e)

    for cm in sem_cms + cms[::-1]:
        cm.__exit__(None, None, None)
    return nc


_NC = None


def kernel(L0_attn_x, L0_attn_Wq, L0_attn_Wk, L0_attn_Wv,
           L0_ffn_router_w, L0_ffn_router_b, L0_ffn_expert_w):
    global _NC
    x = np.ascontiguousarray(np.asarray(L0_attn_x, dtype=np.float32))
    wqT = np.ascontiguousarray(np.asarray(L0_attn_Wq, np.float32).T)
    wkT = np.ascontiguousarray(np.asarray(L0_attn_Wk, np.float32).T)
    wvT = np.ascontiguousarray(np.asarray(L0_attn_Wv, np.float32).T)
    rwT = np.ascontiguousarray(np.asarray(L0_ffn_router_w, np.float32).T)
    rb = np.ascontiguousarray(np.broadcast_to(
        np.asarray(L0_ffn_router_b, np.float32), (128, NE)))
    ew = np.ascontiguousarray(np.asarray(L0_ffn_expert_w, np.float32))
    idb = np.eye(128, dtype=BF16)
    idf = np.eye(128, dtype=np.float32)
    oz = np.zeros((128, 64), dtype=BF16)
    oz[:, 0] = 1.0

    xfm = [np.ascontiguousarray(x[n].T) for n in range(4)]
    in_maps = []
    for c in range(8):
        n, h = c // 2, c % 2
        in_maps.append({
            "xfm": xfm[n],
            "xqf": np.ascontiguousarray(xfm[n][:, h * TOK:(h + 1) * TOK]),
            "xtm": np.ascontiguousarray(x[n, h * TOK:(h + 1) * TOK, :]),
            "wqT": wqT, "wkT": wkT, "wvT": wvT,
            "rwT": rwT, "rb": rb, "ew": ew,
            "idb": idb, "idf": idf, "oz": oz,
        })
    if _NC is None:
        _NC = build()
    res = run_bass_kernel_spmd(_NC, in_maps, core_ids=list(range(8)))
    out = np.concatenate([r["y"] for r in res.results], axis=0)
    return out.reshape(4, SQ, D)


# version 13
# speedup vs baseline: 1.0055x; 1.0055x over previous
import numpy as np
import ml_dtypes
import concourse.bass as bass
import concourse.mybir as mybir
from concourse.bass_utils import run_bass_kernel_spmd

dt = mybir.dt
AF = mybir.ActivationFunctionType
ALU = mybir.AluOpType
AX = mybir.AxisListType
BF16 = ml_dtypes.bfloat16

D = 1024
SQ = 4096
TOK = 2048
NH = 16
HD = 64
NE = 16
CH = 8
NKT = 32
NTT = 16
NPAIR = 8
NBLK = 2
BT = NTT // NBLK
EPS_LN = 1e-5
EPS_MOE = 1e-10


class Ctr:
    def __init__(self):
        self.v = 0


def build():
    NKT = SQ // 128
    NTT = TOK // 128
    BT = NTT // NBLK
    nc = bass.Bass("TRN2", target_bir_lowering=False)

    d_xfm = nc.dram_tensor("xfm", [D, SQ], dt.float32, kind="ExternalInput").ap()
    d_xqf = nc.dram_tensor("xqf", [D, TOK], dt.float32, kind="ExternalInput").ap()
    d_xtm = nc.dram_tensor("xtm", [TOK, D], dt.float32, kind="ExternalInput").ap()
    d_wqT = nc.dram_tensor("wqT", [D, D], dt.float32, kind="ExternalInput").ap()
    d_wkT = nc.dram_tensor("wkT", [D, D], dt.float32, kind="ExternalInput").ap()
    d_wvT = nc.dram_tensor("wvT", [D, D], dt.float32, kind="ExternalInput").ap()
    d_rwT = nc.dram_tensor("rwT", [D, NE], dt.float32, kind="ExternalInput").ap()
    d_rb = nc.dram_tensor("rb", [128, NE], dt.float32, kind="ExternalInput").ap()
    d_ew = nc.dram_tensor("ew", [NE, D, D], dt.float32, kind="ExternalInput").ap()
    d_idb = nc.dram_tensor("idb", [128, 128], dt.bfloat16, kind="ExternalInput").ap()
    d_idf = nc.dram_tensor("idf", [128, 128], dt.float32, kind="ExternalInput").ap()
    d_oz = nc.dram_tensor("oz", [128, 64], dt.bfloat16, kind="ExternalInput").ap()
    d_y = nc.dram_tensor("y", [TOK, D], dt.float32, kind="ExternalOutput").ap()
    d_ewb = nc.dram_tensor("ewb", [NE, D, D], dt.bfloat16).ap()

    v_xfm = d_xfm.rearrange("(c p) t -> p c t", c=CH)
    v_xqf = d_xqf.rearrange("(c p) t -> p c t", c=CH)
    v_wT = {"K": d_wkT.rearrange("(c p) j -> p c j", c=CH),
            "V": d_wvT.rearrange("(c p) j -> p c j", c=CH),
            "Q": d_wqT.rearrange("(c p) j -> p c j", c=CH)}
    v_rwT = d_rwT.rearrange("(c p) e -> p c e", c=CH)

    cms = []

    def alloc(name, shape, dtype, psum=False):
        cm = nc.psum_tensor(name, shape, dtype) if psum else nc.sbuf_tensor(name, shape, dtype)
        t = cm.__enter__()
        cms.append(cm)
        return t

    idb = alloc("s_idb", [128, 128], dt.bfloat16)
    idf = alloc("s_idf", [128, 128], dt.float32)
    ozb = alloc("s_ozb", [128, 64], dt.bfloat16)
    rbr = alloc("s_rbr", [128, NE], dt.float32)
    rwTb = alloc("s_rwTb", [128, CH * NE], dt.bfloat16)
    cb3 = alloc("s_cb3", [128, 1], dt.float32)
    rzpre = alloc("s_rzpre", [128, NTT * NH], dt.float32)
    rz = alloc("s_rz", [128, NTT * NH], dt.float32)
    gates = alloc("s_gates", [128, NTT * NE], dt.float32)
    eps2 = alloc("s_eps2", [128, NTT], dt.float32)
    tA = alloc("s_tA", [128, 1], dt.float32)
    tB = alloc("s_tB", [128, 1], dt.float32)
    tC = alloc("s_tC", [128, 1], dt.float32)
    tD = alloc("s_tD", [128, 1], dt.float32)
    tE = alloc("s_tE", [128, 1], dt.float32)
    QO = alloc("s_QO", [128, CH * TOK], dt.bfloat16)
    RA = alloc("s_RA", [128, CH * SQ], dt.bfloat16)
    RB = alloc("s_RB", [128, NKT * D], dt.bfloat16)
    RC = alloc("s_RC", [128, 16384], dt.bfloat16)
    PS = alloc("s_PS", [128, 4096], dt.float32, psum=True)

    RAf = RA[:, :].bitcast(dt.float32)
    RBf = RB[:, :].bitcast(dt.float32)

    def bank(i, lo=0, hi=512):
        return PS[:, i * 512 + lo: i * 512 + hi]

    def xstage(par):
        return RC[:, par * 4096:(par + 1) * 4096]

    wT = RC[:, 8192:16384]

    def pbuf(par):
        return RC[:, par * 2048:(par + 1) * 2048]

    zs = RC[:, 8192:10240].bitcast(dt.float32)

    def x0stage(par):
        return RC[:, 8192 + par * 2048: 8192 + (par + 1) * 2048].bitcast(dt.float32)

    u_f32 = RC[:, 4096:6144].bitcast(dt.float32)
    usq_f32 = RC[:, 6144:8192].bitcast(dt.float32)
    x1b = RC[:, 12288:13312]
    e16 = RC[:, 13312:13344].bitcast(dt.float32)
    lgt = RC[:, 13344:13376].bitcast(dt.float32)
    mx8 = RC[:, 13376:13392].bitcast(dt.float32)
    sel = RC[:, 13392:13424].bitcast(dt.float32)
    t16 = RC[:, 13424:13456].bitcast(dt.float32)
    t16b = RC[:, 13456:13488].bitcast(dt.float32)

    def ot32(par):
        return RC[:, par * 2048:(par + 1) * 2048].bitcast(dt.float32)

    def accA(tt):
        return RAf[:, 8192 + tt * 1024: 8192 + (tt + 1) * 1024]

    def accB(tt):
        return RBf[:, 8192 + tt * 1024: 8192 + (tt + 1) * 1024]

    def x1slot(tt):
        return RAf[:, tt * 1024: (tt + 1) * 1024]

    def wbuf(par):
        return RB[:, par * 8192:(par + 1) * 8192]

    sem_cms = []

    def mksem(name):
        cm = nc.semaphore(name)
        s = cm.__enter__()
        sem_cms.append(cm)
        return s

    gC = mksem("gC")
    gEW = mksem("gEW")
    gX = [mksem("gX0"), mksem("gX1")]
    gW = [mksem("gW0"), mksem("gW1")]
    sX = [mksem("sX0"), mksem("sX1")]
    sO = [mksem("sO0"), mksem("sO1")]
    tS = mksem("tS")
    tV = mksem("tV")
    sE = mksem("sE")
    vE = mksem("vE")
    cgC, ctS, ctV, csE, cvE = (Ctr() for _ in range(5))
    cgX = [Ctr(), Ctr()]
    cgW = [Ctr(), Ctr()]
    csX = [Ctr(), Ctr()]
    csO = [Ctr(), Ctr()]

    SC = {k: [] for k in ("sync", "gpsimd", "tensor", "vector", "scalar")}

    def q(eng, fn):
        SC[eng].append(fn)

    for (dst, src) in ((idb[:, :], d_idb), (idf[:, :], d_idf), (ozb[:, :], d_oz),
                       (rbr[:, :], d_rb),
                       (rwTb[:, :].rearrange("p (c e) -> p c e", c=CH), v_rwT)):
        def f(g, dst=dst, src=src):
            g.dma_start(out=dst, in_=src).then_inc(gC, 16)
        q("gpsimd", f)
        cgC.v += 16

    consts_done = cgC.v

    def f(v):
        v.wait_ge(gC, consts_done)
        v.memset(cb3[:, :], -3.0)
        v.memset(rzpre[:, :], 0.0)
    q("vector", f)

    def proj_pass(kind, nsect):
        if kind == "V":
            wtile = QO[:, 0:8192]
            wt_gate = 0
        else:
            wtile = wT
            wt_gate = ctS.v if kind == "Q" else 0

        def fw(g, wsrc=v_wT[kind], wt_gate=wt_gate, wtile=wtile):
            if wt_gate > 0:
                g.wait_ge(tS, wt_gate)
            g.dma_start(out=wtile.rearrange("p (c j) -> p c j", c=CH),
                        in_=wsrc).then_inc(gW[0], 16)
        q("gpsimd", fw)
        cgW[0].v += 16
        w_done = cgW[0].v
        xsrc = v_xqf if kind == "Q" else v_xfm

        for s in range(nsect):
            par = s % 2
            ts_need = max(0, ctS.v - 8)

            def fx(g, s=s, par=par, ts_need=ts_need, xsrc=xsrc):
                if ts_need > 0:
                    g.wait_ge(tS, ts_need)
                g.dma_start(out=xstage(par).rearrange("p (c t) -> p c t", c=CH),
                            in_=xsrc[:, :, s * 512:(s + 1) * 512]).then_inc(gX[par], 16)
            q("gpsimd", fx)
            cgX[par].v += 16
            x_done = cgX[par].v

            for u in range(8):
                b = ctS.v % 4
                ve_need = ctS.v - 3

                if kind in ("K", "Q"):
                    jc = u

                    def ft(t, jc=jc, b=b, par=par, x_done=x_done, w_done=w_done,
                           ve_need=ve_need):
                        t.wait_ge(gW[0], w_done)
                        t.wait_ge(gX[par], x_done)
                        if ve_need > 0:
                            t.wait_ge(vE, ve_need)
                        for dc in range(CH):
                            ins = t.matmul(
                                bank(b),
                                wtile[:, dc * 1024 + jc * 128: dc * 1024 + (jc + 1) * 128],
                                xstage(par)[:, dc * 512:(dc + 1) * 512],
                                start=(dc == 0), stop=(dc == CH - 1))
                        ins.then_inc(tS, 1)
                    if kind == "K":
                        dst = RA[:, jc * SQ + s * 512: jc * SQ + (s + 1) * 512]
                    else:
                        dst = QO[:, jc * TOK + s * 512: jc * TOK + (s + 1) * 512]
                else:
                    tt, jh = u // 2, u % 2
                    kt = s * 4 + tt

                    def ft(t, tt=tt, jh=jh, b=b, par=par, x_done=x_done,
                           w_done=w_done, ve_need=ve_need):
                        t.wait_ge(gW[0], w_done)
                        t.wait_ge(gX[par], x_done)
                        if ve_need > 0:
                            t.wait_ge(vE, ve_need)
                        for dc in range(CH):
                            ins = t.matmul(
                                bank(b),
                                xstage(par)[:, dc * 512 + tt * 128:
                                            dc * 512 + (tt + 1) * 128],
                                wtile[:, dc * 1024 + jh * 512: dc * 1024 + (jh + 1) * 512],
                                start=(dc == 0), stop=(dc == CH - 1))
                        ins.then_inc(tS, 1)
                    dst = RB[:, kt * D + jh * 512: kt * D + (jh + 1) * 512]

                q("tensor", ft)
                ctS.v += 1
                te_after = ctS.v

                def fv(v, dst=dst, b=b, te_after=te_after):
                    v.wait_ge(tS, te_after)
                    v.tensor_copy(dst, bank(b)).then_inc(vE, 1)
                q("vector", fv)
                cvE.v += 1

    proj_pass("K", SQ // 512)
    proj_pass("V", SQ // 512)
    proj_pass("Q", TOK // 512)
    p1_ve = cvE.v
    p1_ts = ctS.v

    for e in range(NE):
        def few(g, e=e):
            g.dma_start(out=d_ewb[e], in_=d_ew[e]).then_inc(gEW, 16)
        q("gpsimd", few)
    ew_cast_done = NE * 16

    fve_cnt = {}
    fvz_cnt = {}
    pv_cnt = {}
    z_cnt = {}

    def emit_pv(t, c, kcp, first, last):
        P = pbuf(kcp % 4)
        for head in range(2):
            hh = 2 * c + head
            orow = slice(head * 64, (head + 1) * 64)
            for qc in range(2):
                ins = t.matmul(
                    PS[orow, 2048 + qc * 512: 2048 + (qc + 1) * 512],
                    RB[:, kcp * D + hh * 64: kcp * D + (hh + 1) * 64],
                    P[:, head * 1024 + qc * 512: head * 1024 + (qc + 1) * 512],
                    start=first, stop=last, tile_position=(0, head * 64),
                    skip_group_check=True)
        return ins

    def emit_z(t, c, kcp, first, last):
        P = pbuf(kcp % 4)
        for head in range(2):
            orow = slice(head * 64, (head + 1) * 64)
            for qc in range(2):
                ins = t.matmul(
                    PS[orow, 3072 + qc * 512: 3072 + (qc + 1) * 512],
                    ozb[:, :],
                    P[:, head * 1024 + qc * 512: head * 1024 + (qc + 1) * 512],
                    start=first, stop=last, tile_position=(0, head * 64),
                    skip_group_check=True)
        return ins

    def emit_ftz(p_prev):
        def ftz(t, zev=fve_cnt[p_prev]):
            t.wait_ge(vE, zev)
            for sl in range(8):
                ins = t.transpose(PS[:, 3072 + sl * 128: 3072 + (sl + 1) * 128],
                                  zs[:, sl * 128:(sl + 1) * 128], idf[:, :])
            ins.then_inc(tV, 1)
        q("tensor", ftz)
        ctV.v += 1
        ztr = ctV.v

        def fvz(v, c=p_prev % NPAIR, h=p_prev // NPAIR, ztr=ztr):
            v.wait_ge(tV, ztr)
            for sl in range(8):
                tile = h * 8 + sl
                v.tensor_copy(rzpre[:, tile * NH + 2 * c: tile * NH + 2 * c + 1],
                              PS[:, 3072 + sl * 128: 3072 + sl * 128 + 1])
                ins = v.tensor_copy(
                    rzpre[:, tile * NH + 2 * c + 1: tile * NH + 2 * c + 2],
                    PS[:, 3072 + sl * 128 + 64: 3072 + sl * 128 + 65])
            ins.then_inc(vE, 1)
        q("vector", fvz)
        cvE.v += 1
        fvz_cnt[p_prev] = cvE.v

    pvl_exp_need = {}

    def emit_zl(p_prev):
        cc = p_prev % NPAIR

        def ftZl(t, cc=cc, exp_need=pvl_exp_need[p_prev]):
            t.wait_ge(sE, exp_need)
            for kcp in (NKT - 3, NKT - 2, NKT - 1):
                ins = emit_z(t, cc, kcp, False, kcp == NKT - 1)
            ins.then_inc(tV, 1)
        q("tensor", ftZl)
        ctV.v += 1
        for kcp in (NKT - 3, NKT - 2, NKT - 1):
            z_cnt[(p_prev, kcp)] = ctV.v

    def emit_pvl_fve(p_prev):
        cc = p_prev % NPAIR

        def ftPVl(t, cc=cc, exp_need=pvl_exp_need[p_prev]):
            t.wait_ge(sE, exp_need)
            emit_pv(t, cc, NKT - 1, False, True).then_inc(tV, 1)
        q("tensor", ftPVl)
        ctV.v += 1
        pv_cnt[(p_prev, NKT - 1)] = ctV.v
        pvz_done = ctV.v
        qc0p = (p_prev // NPAIR) * 1024

        def fve(v, cc=cc, qc0=qc0p, pvz_done=pvz_done):
            v.wait_ge(tV, pvz_done)
            v.tensor_copy(QO[:, cc * TOK + qc0: cc * TOK + qc0 + 1024],
                          PS[:, 2048:3072])
            ins = v.tensor_copy(zs[:, :], PS[:, 3072:4096])
            ins.then_inc(vE, 1)
        q("vector", fve)
        cvE.v += 1
        fve_cnt[p_prev] = cvE.v

    for h in range(2):
        for c in range(NPAIR):
            p = h * NPAIR + c
            qc0 = h * 1024

            for kc in range(NKT):
                exp_need = csE.v
                first_pair_wait = p1_ve if (p == 0 and kc == 0) else 0

                def ftS(t, c=c, kc=kc, qc0=qc0, exp_need=exp_need,
                        fw=first_pair_wait):
                    if fw > 0:
                        t.wait_ge(vE, fw)
                        t.wait_ge(gC, consts_done)
                    for head in range(2):
                        t.wait_ge(sE, max(0, exp_need - 1 + head))
                        rows = slice(head * 64, (head + 1) * 64)
                        for qc in range(2):
                            ins = t.matmul(
                                PS[:, head * 1024 + qc * 512:
                                   head * 1024 + (qc + 1) * 512],
                                RA[rows, c * SQ + kc * 128: c * SQ + (kc + 1) * 128],
                                QO[rows, c * TOK + qc0 + qc * 512:
                                   c * TOK + qc0 + (qc + 1) * 512],
                                start=True, stop=True, tile_position=(head * 64, 0))
                        if head == 0:
                            ins.then_inc(tS, 1)
                    ins.then_inc(tS, 1)
                q("tensor", ftS)
                ctS.v += 2
                s_done = ctS.v

                if p > 0:
                    if kc == 0:
                        emit_zl(p - 1)
                    elif kc == 1:
                        emit_pvl_fve(p - 1)
                    elif kc == 2:
                        emit_ftz(p - 1)

                if kc > 0:
                    kcp = kc - 1
                    ve_need = fve_cnt.get(p - 1, 0) if kcp == 0 else 0

                    def ftPV(t, c=c, kcp=kcp, exp_need=exp_need, ve_need=ve_need):
                        t.wait_ge(sE, exp_need)
                        if ve_need > 0:
                            t.wait_ge(vE, ve_need)
                        emit_pv(t, c, kcp, kcp == 0, False).then_inc(tV, 1)
                    q("tensor", ftPV)
                    ctV.v += 1
                    pv_cnt[(p, kcp)] = ctV.v

                zl = []
                if kc >= 3:
                    zl = [kc - 3]
                if zl:
                    ve_need = fvz_cnt.get(p - 1, 0) if zl[0] == 0 else 0

                    def ftZ(t, c=c, zl=tuple(zl), ve_need=ve_need):
                        if ve_need > 0:
                            t.wait_ge(vE, ve_need)
                        for kcp in zl:
                            ins = emit_z(t, c, kcp, kcp == 0, False)
                        ins.then_inc(tV, 1)
                    q("tensor", ftZ)
                    ctV.v += 1
                    for kcp in zl:
                        z_cnt[(p, kcp)] = ctV.v

                def fse(sc, s_done=s_done, kc=kc,
                        pv_need=(z_cnt.get((p, kc - 4), 0) if kc >= 4 else 0)):
                    if pv_need > 0:
                        sc.wait_ge(tV, pv_need)
                    P = pbuf(kc % 4)
                    sc.wait_ge(tS, s_done - 1)
                    sc.activation(P[:, 0:1024], PS[:, 0:1024], AF.Exp,
                                  bias=cb3[:, :], scale=0.125).then_inc(sE, 1)
                    sc.wait_ge(tS, s_done)
                    sc.activation(P[:, 1024:2048], PS[:, 1024:2048], AF.Exp,
                                  bias=cb3[:, :], scale=0.125).then_inc(sE, 1)
                q("scalar", fse)
                csE.v += 2

            pvl_exp_need[p] = csE.v

    emit_zl(2 * NPAIR - 1)
    emit_pvl_fve(2 * NPAIR - 1)
    emit_ftz(2 * NPAIR - 1)

    p2_ve = cvE.v
    p2_tv = ctV.v

    def fvr(v, p2_ve=p2_ve):
        v.wait_ge(vE, p2_ve)
        v.reciprocal(rz[:, :], rzpre[:, :]).then_inc(vE, 1)
    q("vector", fvr)
    cvE.v += 1

    fvu_counts = {}
    fvl_counts = {}
    fvt_counts = {}
    ffn_mm_last = {}
    ffn_pb_last = {}

    def p3_tile(blk, tt, mid=None):
        t_i = blk * BT + tt
        hh = t_i // 8
        par = t_i % 2
        pcol = par * 1024

        if t_i >= 2:
            ve_need = fvu_counts[t_i - 2]
        else:
            ve_need = 0
        ts_need = p1_ts if t_i < 2 else 0
        tv_need = p2_tv if t_i < 2 else 0

        def fx0(sy, t_i=t_i, par=par, ve_need=ve_need, ts_need=ts_need,
                tv_need=tv_need):
            if ts_need > 0:
                sy.wait_ge(tS, ts_need)
            if tv_need > 0:
                sy.wait_ge(tV, tv_need)
            if ve_need > 0:
                sy.wait_ge(vE, ve_need)
            sy.dma_start(out=x0stage(par),
                         in_=d_xtm[t_i * 128:(t_i + 1) * 128, :]).then_inc(sX[par], 16)
        q("sync", fx0)
        csX[par].v += 16
        x0_done = csX[par].v

        tr_need = fvl_counts.get(t_i - 2, p2_ve + 1 if t_i < 2 else 0)

        def ftr(t, t_i=t_i, pcol=pcol, tr_need=tr_need, first=(t_i == 0)):
            if first:
                t.wait_ge(vE, p2_ve + 1)
            if tr_need > 0:
                t.wait_ge(vE, tr_need)
            q_lo = t_i * 128
            for cc in range(CH):
                ins = t.matmul(PS[:, pcol + cc * 128: pcol + (cc + 1) * 128],
                               QO[:, cc * TOK + q_lo: cc * TOK + q_lo + 128],
                               idb[:, :], start=True, stop=True)
            ins.then_inc(tS, 1)
        q("tensor", ftr)
        ctS.v += 1
        tr_done = ctS.v
        if mid is not None:
            mid()

        def fvu(v, t_i=t_i, pcol=pcol, par=par, tr_done=tr_done, x0_done=x0_done):
            v.wait_ge(tS, tr_done)
            v.wait_ge(sX[par], x0_done)
            for sg in range(16):
                v.scalar_tensor_tensor(
                    u_f32[:, sg * 64:(sg + 1) * 64],
                    PS[:, pcol + sg * 64: pcol + (sg + 1) * 64],
                    rz[:, t_i * NH + sg: t_i * NH + sg + 1],
                    x0stage(par)[:, sg * 64:(sg + 1) * 64],
                    ALU.mult, ALU.add)
            v.drain()
            v.scalar_tensor_tensor(usq_f32[:, :], u_f32[:, :], 1.0, u_f32[:, :],
                                   ALU.mult, ALU.mult, accum_out=tA[:, :])
            v.reduce_sum(tB[:, :], u_f32[:, :], axis=AX.X)
            v.drain()
            v.tensor_scalar(tC[:, :], tB[:, :], 1.0 / D, None, ALU.mult)
            v.tensor_scalar(tD[:, :], tA[:, :], 1.0 / D, None, ALU.mult)
            v.drain()
            v.scalar_tensor_tensor(tE[:, :], tC[:, :], 1.0, tC[:, :],
                                   ALU.mult, ALU.mult)
            v.drain()
            v.tensor_tensor(tA[:, :], tD[:, :], tE[:, :], ALU.subtract)
            v.drain()
            v.tensor_scalar(tB[:, :], tA[:, :], EPS_LN, None,
                            ALU.add).then_inc(vE, 1)
        q("vector", fvu)
        cvE.v += 1
        fvu_counts[t_i] = cvE.v
        u_done = cvE.v

        def fsr(sc, u_done=u_done):
            sc.wait_ge(vE, u_done)
            sc.activation(tE[:, :], tB[:, :], AF.Ln, bias=0.0, scale=1.0)
            sc.drain()
            sc.activation(tD[:, :], tE[:, :], AF.Exp, bias=0.0,
                          scale=-0.5).then_inc(sE, 1)
        q("scalar", fsr)
        csE.v += 1
        r_done = csE.v

        def fva(v, tt=tt, r_done=r_done):
            v.wait_ge(sE, r_done)
            v.tensor_scalar(x1slot(tt), u_f32[:, :], tC[:, :], tD[:, :],
                            ALU.subtract, ALU.mult).then_inc(vE, 1)
        q("vector", fva)
        cvE.v += 1
        av_done = cvE.v

        def fsc(sc, tt=tt, av_done=av_done):
            sc.wait_ge(vE, av_done)
            sc.copy(x1b[:, :], x1slot(tt)).then_inc(sE, 1)
        q("scalar", fsc)
        csE.v += 1
        a_done = csE.v

        def ftx(t, pcol=pcol, a_done=a_done):
            t.wait_ge(sE, a_done)
            for cc in range(CH):
                ins = t.matmul(PS[:, pcol + cc * 128: pcol + (cc + 1) * 128],
                               x1b[:, cc * 128:(cc + 1) * 128], idb[:, :],
                               start=True, stop=True)
            ins.then_inc(tS, 1)
        q("tensor", ftx)
        ctS.v += 1
        x1t_done = ctS.v

        def fscx(sc, t_i=t_i, pcol=pcol, x1t_done=x1t_done):
            sc.wait_ge(tS, x1t_done)
            sc.copy(
                QO[:, :].rearrange("p (c t) -> p c t", c=CH)[:, :, t_i * 128:(t_i + 1) * 128],
                PS[:, pcol: pcol + 1024].rearrange("p (c t) -> p c t", c=CH),
            ).then_inc(sE, 1)
        q("scalar", fscx)
        csE.v += 1
        x1f_done = csE.v

        def ftro(t, t_i=t_i, pcol=pcol, x1f_done=x1f_done):
            t.wait_ge(sE, x1f_done)
            for dc in range(CH):
                ins = t.matmul(PS[:, pcol: pcol + NE],
                               QO[:, dc * TOK + t_i * 128: dc * TOK + (t_i + 1) * 128],
                               rwTb[:, dc * NE:(dc + 1) * NE],
                               start=(dc == 0), stop=(dc == CH - 1))
            ins.then_inc(tS, 1)
        q("tensor", ftro)
        ctS.v += 1
        ro_done = ctS.v

        def fvl(v, pcol=pcol, ro_done=ro_done):
            v.wait_ge(tS, ro_done)
            v.tensor_tensor(lgt[:, :], PS[:, pcol: pcol + NE], rbr[:, :], ALU.add)
            v.drain()
            v.reduce_max(tA[:, :], lgt[:, :], axis=AX.X,
                         negate=True).then_inc(vE, 1)
        q("vector", fvl)
        cvE.v += 1
        fvl_counts[t_i] = cvE.v
        l_done = cvE.v

        def fsx(sc, l_done=l_done):
            sc.wait_ge(vE, l_done)
            sc.activation(e16[:, :], lgt[:, :], AF.Exp, bias=tA[:, :],
                          scale=1.0, accum_out=tB[:, :]).then_inc(sE, 1)
        q("scalar", fsx)
        csE.v += 1
        e_done = csE.v

        def fvt(v, t_i=t_i, e_done=e_done):
            v.wait_ge(sE, e_done)
            v.reciprocal(tC[:, :], tB[:, :])
            v.max(mx8[:, :], e16[:, :])
            v.drain()
            v.tensor_scalar(t16[:, 0:6], mx8[:, 0:6], tC[:, :], None, ALU.mult)
            v.drain()
            v.reduce_sum(tA[:, :], t16[:, 0:6], axis=AX.X)
            v.drain()
            v.tensor_scalar(tB[:, :], tA[:, :], EPS_MOE, None, ALU.add)
            v.drain()
            v.reciprocal(tD[:, :], tB[:, :])
            v.drain()
            v.tensor_scalar(t16b[:, 0:6], t16[:, 0:6], tD[:, :], None, ALU.mult)
            v.drain()
            v.reduce_sum(tA[:, :], t16b[:, 0:6], axis=AX.X)
            v.drain()
            v.tensor_scalar(eps2[:, t_i:t_i + 1], tA[:, :], EPS_MOE, None,
                            ALU.mult)
            v.tensor_scalar(tB[:, :], tA[:, :], EPS_MOE, None, ALU.max)
            v.drain()
            v.reciprocal(tE[:, :], tB[:, :])
            v.drain()
            v.tensor_tensor(tA[:, :], tD[:, :], tE[:, :], ALU.mult)
            v.drain()
            v.tensor_tensor(tB[:, :], tA[:, :], tC[:, :], ALU.mult)
            v.tensor_scalar(sel[:, :], e16[:, :], mx8[:, 5:6], None, ALU.is_ge)
            v.drain()
            v.tensor_scalar(t16[:, 0:16], e16[:, :], tB[:, :], None, ALU.mult)
            v.drain()
            v.tensor_tensor(gates[:, t_i * NE:(t_i + 1) * NE], sel[:, :],
                            t16[:, 0:16], ALU.mult).then_inc(vE, 1)
        q("vector", fvt)
        cvE.v += 1
        fvt_counts[t_i] = cvE.v

    def ffn_wload(blk, e):
        gi = blk * NE + e
        if gi < 2:
            wl_tv, wl_ts = p2_tv, 0
        else:
            wl_tv, wl_ts = 0, ffn_mm_last[gi - 2]

        def fwl(g, e=e, wl_tv=wl_tv, wl_ts=wl_ts, par=gi % 2, first=(gi < 2)):
            if first:
                g.wait_ge(gEW, ew_cast_done)
            if wl_tv > 0:
                g.wait_ge(tV, wl_tv)
            if wl_ts > 0:
                g.wait_ge(tS, wl_ts)
            g.dma_start(out=wbuf(par).rearrange("p (c j) -> p c j", c=CH),
                        in_=d_ewb[e].rearrange("(c p) j -> p c j", c=CH)
                        ).then_inc(gW[par], 16)
        q("gpsimd", fwl)
        cgW[gi % 2].v += 16
        return cgW[gi % 2].v

    def ffn_unit(blk, e, tt, w_done):
        gi = blk * NE + e
        t_i = blk * BT + tt
        pb = 2048 + (tt % 2) * 1024
        ve_need = ffn_pb_last.get(tt % 2, 0)
        gate_need = fvt_counts[t_i] if e <= 1 else 0

        def ftf(t, t_i=t_i, pb=pb, w_done=w_done, first=(tt == 0),
                par=gi % 2, ve_need=ve_need, gate_need=gate_need):
            if first:
                t.wait_ge(gW[par], w_done)
            if gate_need > 0:
                t.wait_ge(vE, gate_need)
            if ve_need > 0:
                t.wait_ge(vE, ve_need)
            for jh in range(2):
                for dc in range(CH):
                    ins = t.matmul(
                        PS[:, pb + jh * 512: pb + (jh + 1) * 512],
                        QO[:, dc * TOK + t_i * 128: dc * TOK + (t_i + 1) * 128],
                        wbuf(par)[:, dc * 1024 + jh * 512: dc * 1024 + (jh + 1) * 512],
                        start=(dc == 0), stop=(dc == CH - 1))
            ins.then_inc(tS, 1)
        q("tensor", ftf)
        ctS.v += 1
        mm_done = ctS.v

        def fvf(v, e=e, tt=tt, t_i=t_i, pb=pb, mm_done=mm_done):
            v.wait_ge(tS, mm_done)
            g = gates[:, t_i * NE + e: t_i * NE + e + 1]
            if e == 0:
                v.tensor_scalar(accA(tt), PS[:, pb: pb + 1024], g, None,
                                ALU.mult).then_inc(vE, 1)
            elif e % 2 == 1:
                v.scalar_tensor_tensor(accB(tt), PS[:, pb: pb + 1024], g,
                                       accA(tt), ALU.mult, ALU.add).then_inc(vE, 1)
            else:
                v.scalar_tensor_tensor(accA(tt), PS[:, pb: pb + 1024], g,
                                       accB(tt), ALU.mult, ALU.add).then_inc(vE, 1)
        q("vector", fvf)
        cvE.v += 1
        ffn_pb_last[tt % 2] = cvE.v

    def ln2_tile(blk, tt):
        t_i = blk * BT + tt
        par = tt % 2
        sd_need = csO[par].v

        def fv2(v, tt=tt, t_i=t_i, par=par, sd_need=sd_need, sw=cvE.v):
            v.wait_ge(vE, sw)
            if sd_need > 0:
                v.wait_ge(sO[par], sd_need)
            v.scalar_tensor_tensor(u_f32[:, :], accB(tt),
                                   eps2[:, t_i:t_i + 1], x1slot(tt),
                                   ALU.add, ALU.add, accum_out=tB[:, :])
            v.drain()
            v.scalar_tensor_tensor(usq_f32[:, :], u_f32[:, :], 1.0, u_f32[:, :],
                                   ALU.mult, ALU.mult, accum_out=tA[:, :])
            v.drain()
            v.tensor_scalar(tC[:, :], tB[:, :], 1.0 / D, None, ALU.mult)
            v.tensor_scalar(tD[:, :], tA[:, :], 1.0 / D, None, ALU.mult)
            v.drain()
            v.scalar_tensor_tensor(tE[:, :], tC[:, :], 1.0, tC[:, :],
                                   ALU.mult, ALU.mult)
            v.drain()
            v.tensor_tensor(tA[:, :], tD[:, :], tE[:, :], ALU.subtract)
            v.drain()
            v.tensor_scalar(tB[:, :], tA[:, :], EPS_LN, None,
                            ALU.add).then_inc(vE, 1)
        q("vector", fv2)
        cvE.v += 1
        u2_done = cvE.v

        def fs2(sc, u2_done=u2_done):
            sc.wait_ge(vE, u2_done)
            sc.activation(tE[:, :], tB[:, :], AF.Ln, bias=0.0, scale=1.0)
            sc.drain()
            sc.activation(tD[:, :], tE[:, :], AF.Exp, bias=0.0,
                          scale=-0.5).then_inc(sE, 1)
        q("scalar", fs2)
        csE.v += 1
        r2_done = csE.v

        def fva2(v, par=par, r2_done=r2_done):
            v.wait_ge(sE, r2_done)
            v.tensor_scalar(ot32(par), u_f32[:, :], tC[:, :], tD[:, :],
                            ALU.subtract, ALU.mult).then_inc(vE, 1)
        q("vector", fva2)
        cvE.v += 1
        o_done = cvE.v

        def fso(sy, t_i=t_i, par=par, o_done=o_done):
            sy.wait_ge(vE, o_done)
            sy.dma_start(out=d_y[t_i * 128:(t_i + 1) * 128, :],
                         in_=ot32(par)).then_inc(sO[par], 16)
        q("sync", fso)
        csO[par].v += 16

    for blk in range(NBLK):
        w0 = ffn_wload(blk, 0)
        w1 = ffn_wload(blk, 1)
        for tt in range(BT):
            if tt == 0:
                p3_tile(blk, tt)
            else:
                p3_tile(blk, tt, mid=(lambda tp=tt - 1: (
                    ffn_unit(blk, 0, tp, w0), ffn_unit(blk, 1, tp, w1))))
        ffn_unit(blk, 0, BT - 1, w0)
        ffn_unit(blk, 1, BT - 1, w1)
        ffn_mm_last[blk * NE + 0] = ctS.v
        ffn_mm_last[blk * NE + 1] = ctS.v
        for e in range(2, NE):
            we = ffn_wload(blk, e)
            for tt in range(BT):
                ffn_unit(blk, e, tt, we)
                if e == NE - 1:
                    ln2_tile(blk, tt)
            ffn_mm_last[blk * NE + e] = ctS.v

    fin0, fin1 = csO[0].v, csO[1].v

    def ffin(sy, fin0=fin0, fin1=fin1):
        sy.wait_ge(sO[0], fin0)
        sy.wait_ge(sO[1], fin1)
    q("sync", ffin)

    def gfin(g, fin0=fin0, fin1=fin1):
        g.wait_ge(sO[0], fin0)
        g.wait_ge(sO[1], fin1)
    q("gpsimd", gfin)

    with nc.Block() as block:
        @block.sync
        def _(e):
            for fn in SC["sync"]:
                fn(e)

        @block.gpsimd
        def _(e):
            for fn in SC["gpsimd"]:
                fn(e)

        @block.tensor
        def _(e):
            for fn in SC["tensor"]:
                fn(e)

        @block.vector
        def _(e):
            for fn in SC["vector"]:
                fn(e)

        @block.scalar
        def _(e):
            for fn in SC["scalar"]:
                fn(e)

    for cm in sem_cms + cms[::-1]:
        cm.__exit__(None, None, None)
    return nc


_NC = None


def kernel(L0_attn_x, L0_attn_Wq, L0_attn_Wk, L0_attn_Wv,
           L0_ffn_router_w, L0_ffn_router_b, L0_ffn_expert_w):
    global _NC
    x = np.ascontiguousarray(np.asarray(L0_attn_x, dtype=np.float32))
    wqT = np.ascontiguousarray(np.asarray(L0_attn_Wq, np.float32).T)
    wkT = np.ascontiguousarray(np.asarray(L0_attn_Wk, np.float32).T)
    wvT = np.ascontiguousarray(np.asarray(L0_attn_Wv, np.float32).T)
    rwT = np.ascontiguousarray(np.asarray(L0_ffn_router_w, np.float32).T)
    rb = np.ascontiguousarray(np.broadcast_to(
        np.asarray(L0_ffn_router_b, np.float32), (128, NE)))
    ew = np.ascontiguousarray(np.asarray(L0_ffn_expert_w, np.float32))
    idb = np.eye(128, dtype=BF16)
    idf = np.eye(128, dtype=np.float32)
    oz = np.zeros((128, 64), dtype=BF16)
    oz[:, 0] = 1.0

    xfm = [np.ascontiguousarray(x[n].T) for n in range(4)]
    in_maps = []
    for c in range(8):
        n, h = c // 2, c % 2
        in_maps.append({
            "xfm": xfm[n],
            "xqf": np.ascontiguousarray(xfm[n][:, h * TOK:(h + 1) * TOK]),
            "xtm": np.ascontiguousarray(x[n, h * TOK:(h + 1) * TOK, :]),
            "wqT": wqT, "wkT": wkT, "wvT": wvT,
            "rwT": rwT, "rb": rb, "ew": ew,
            "idb": idb, "idf": idf, "oz": oz,
        })
    if _NC is None:
        _NC = build()
    res = run_bass_kernel_spmd(_NC, in_maps, core_ids=list(range(8)))
    out = np.concatenate([r["y"] for r in res.results], axis=0)
    return out.reshape(4, SQ, D)
